# version 24
# baseline (speedup 1.0000x reference)
import numpy as np
import ml_dtypes
import concourse.bass as bass
import concourse.mybir as mybir
from concourse.bass_utils import run_bass_kernel_spmd
from contextlib import ExitStack

F32 = mybir.dt.float32
BF16 = mybir.dt.bfloat16
U32 = mybir.dt.uint32
AF = mybir.ActivationFunctionType
AX = mybir.AxisListType
OP = mybir.AluOpType

D = 1024
H = 2048
NEXP = 8
NCORE = 8
NTOK = 2 * 16384
NC = NTOK // NCORE
NSUP = 2048
SUPS = NC // NSUP
CHUNKS = NSUP // 128
DC = D // 128
HC = H // 128
CAP = 768
SEGCH = CAP // 128
NSLOT = NEXP * CAP
ROWW = D + 8
BIG = float(1 << 28)


def build_program(debug=False, paranoid=None):
    import os
    if paranoid is None:
        paranoid = bool(os.environ.get("BASS_SIM_FP64"))
    nc = bass.Bass("TRN2", target_bir_lowering=False)

    x_h = nc.dram_tensor("x", [NC, D], F32, kind="ExternalInput").ap()
    wg_h = nc.dram_tensor("wg", [D, NEXP], F32, kind="ExternalInput").ap()
    bg_h = nc.dram_tensor("bg", [NEXP], F32, kind="ExternalInput").ap()
    w1_h = nc.dram_tensor("w1", [NEXP, D, H], F32, kind="ExternalInput").ap()
    b1_h = nc.dram_tensor("b1", [NEXP, H], F32, kind="ExternalInput").ap()
    w2_h = nc.dram_tensor("w2", [NEXP, H, D], F32, kind="ExternalInput").ap()
    b2_h = nc.dram_tensor("b2", [NEXP, D], F32, kind="ExternalInput").ap()
    wet_h = nc.dram_tensor("wet", [NEXP, D, D], F32, kind="ExternalInput").ap()
    be_h = nc.dram_tensor("be", [NEXP, D], F32, kind="ExternalInput").ap()
    wrt_h = nc.dram_tensor("wrt", [D, NEXP], F32, kind="ExternalInput").ap()
    br_h = nc.dram_tensor("br", [NEXP], F32, kind="ExternalInput").ap()
    idb_h = nc.dram_tensor("idb", [128, 128], BF16, kind="ExternalInput").ap()
    eoff_h = nc.dram_tensor("eoffs", [NEXP, 1], F32, kind="ExternalInput").ap()
    out_h = nc.dram_tensor("out", [NC, D], F32, kind="ExternalOutput").ap()
    scratch_kind = "ExternalOutput" if debug else "Internal"
    perm_h = nc.dram_tensor("perm_x", [NSLOT, ROWW], BF16, kind=scratch_kind).ap()
    aperm_h = nc.dram_tensor("attn_perm", [NSLOT, D], F32, kind=scratch_kind).ap()
    if debug:
        dbg_slot_h = nc.dram_tensor("dbg_slot", [128, CHUNKS], U32, kind="ExternalOutput").ap()
        dbg_s_h = nc.dram_tensor("dbg_s", [128, CHUNKS * 16], BF16, kind="ExternalOutput").ap()
        dbg_ap_h = nc.dram_tensor("dbg_ap", [NSLOT, D], F32, kind="ExternalOutput").ap()

    ctx = ExitStack()
    EN = ctx.enter_context
    acc = EN(nc.sbuf_tensor("acc", [128, CHUNKS * D], F32))
    R1 = EN(nc.sbuf_tensor("R1", [128, 36864], BF16))
    xn_tok = R1[:, 0:CHUNKS * ROWW]
    wet_sl = [R1[:, 16512:24704], R1[:, 24704:32896]]
    w1q = [R1[:, 0:4096], R1[:, 4096:8192]]
    w2sb = R1[:, 8192:24576]
    hidseg = R1[:, 24576:36864]
    R2 = EN(nc.sbuf_tensor("R2", [128, 16384], BF16))
    xn_fm = R2[:, 0:16384]
    pfm = R2[:, 0:6144]
    pchunk = [R2[:, 6144 + k * ROWW : 6144 + (k + 1) * ROWW] for k in range(6)]

    s_sb = EN(nc.sbuf_tensor("s_sb", [128, CHUNKS * 16], BF16))
    srT = EN(nc.sbuf_tensor("srT", [16, NSUP], BF16))
    maskb = EN(nc.sbuf_tensor("maskb", [8, NSUP], BF16))
    pfx = EN(nc.sbuf_tensor("pfx", [8, NSUP], F32))
    b1sb = EN(nc.sbuf_tensor("b1sb", [128, NEXP * HC], F32))
    bias_stack = EN(nc.sbuf_tensor("bias_stack", [16, D], BF16))
    gate_w = EN(nc.sbuf_tensor("gate_w", [128, DC * 32], BF16))
    ident = EN(nc.sbuf_tensor("ident", [128, 128], BF16))
    ones_row = EN(nc.sbuf_tensor("ones_row", [1, 128], BF16))
    ones8 = EN(nc.sbuf_tensor("ones8", [8, 1], F32))
    eoff_sb = EN(nc.sbuf_tensor("eoff_sb", [8, 1], F32))
    bgbr = EN(nc.sbuf_tensor("bgbr", [1, 16], BF16))
    consts = EN(nc.sbuf_tensor("consts", [128, 16], F32))
    xst = EN(nc.sbuf_tensor("xst", [128, 2 * D], F32))
    outst = EN(nc.sbuf_tensor("outst", [128, 2 * D], F32))
    xnf32_b = [outst[:, 0:D], outst[:, D:2 * D]]
    xlot_b = [R1[:, 32896:33920], R1[:, 33920:34944]]
    xlofm = R1[:, 34944:35968]
    lnw = EN(nc.sbuf_tensor("lnw", [128, 96], F32))
    gw48 = EN(nc.sbuf_tensor("gw48", [128, 48], F32))
    rwbuf = EN(nc.sbuf_tensor("rwbuf", [128, 48], F32))
    lnw_b = [lnw[:, :], R1[:, 35968:36160].bitcast(F32)]
    gw48_b = [gw48[:, :], R1[:, 36160:36256].bitcast(F32)]
    rwbuf_b = [rwbuf[:, :], R1[:, 36256:36352].bitcast(F32)]
    gwf = EN(nc.sbuf_tensor("gwf", [128, DC * 8], F32))
    wrtf = EN(nc.sbuf_tensor("wrtf", [128, DC * 8], F32))
    slotf = EN(nc.sbuf_tensor("slotf", [128, CHUNKS], F32))
    slotu = EN(nc.sbuf_tensor("slotu", [128, CHUNKS], U32))
    topw_sc = EN(nc.sbuf_tensor("topw_sc", [128, 8], F32))
    b1st = R1[0:16, 0:NEXP * 128]
    bgbrst = lnw[0:1, 80:96]
    psA = EN(nc.psum_tensor("psA", [128, 1024], F32))
    psB = EN(nc.psum_tensor("psB", [128, 1024], F32))
    psC = EN(nc.psum_tensor("psC", [128, 1024], F32))
    psD = EN(nc.psum_tensor("psD", [128, 1024], F32))
    sem = {k: EN(nc.semaphore("sem_" + k)) for k in ("w", "x", "pe", "act", "dve", "st")}

    ops = {k: [] for k in ("te", "ve", "sc", "gp", "sy")}
    cnt = {k: 0 for k in ("w", "x", "pe", "act", "dve", "st")}

    def I(eng, fn):
        ops[eng].append(["call", fn])

    def W(eng, sname, val):
        if val and val > 0:
            ops[eng].append(["wait", sem[sname], val])

    def INC(eng, sname, amt=1):
        cnt[sname] += amt
        last = ops[eng][-1]
        assert last[0] == "call", f"INC after non-call on {eng}"
        last[0] = "inccall"
        last.append(sem[sname])
        last.append(amt)
        last.append(cnt[sname])
        return cnt[sname]

    def DMA(eng, dst, src, sname, **kw):
        cnt[sname] += 16
        ops[eng].append(["dma", dst, src, sem[sname], kw])
        return cnt[sname]

    def IDMA(eng, dst, dst_off, src, src_off, sname, **kw):
        cnt[sname] += 16
        ops[eng].append(["idma", dst, dst_off, src, src_off, sem[sname], kw])
        return cnt[sname]

    spacer_i = [0]

    def SP():
        c = 4 + spacer_i[0] % 4
        spacer_i[0] += 1
        I("ve", lambda E, c=c: E.tensor_copy(consts[:, c : c + 1], consts[:, 3:4]))

    def VSYNC():
        c = INC("ve", "dve")
        W("ve", "dve", c)
        return c

    w2v = w2sb.rearrange("p (hc d) -> p hc d", d=D)

    def dma_w1q(e, q, buf):
        return DMA("gp", w1q[buf].rearrange("p (dc h) -> p dc h", h=512),
                   w1_h[e].rearrange("(dc p) h -> p dc h", p=128)[:, :, q * 512 : (q + 1) * 512],
                   "w", max_dma_last_dim=2048)

    def dma_w2(e):
        return DMA("gp", w2v, w2_h[e].rearrange("(hc p) d -> p hc d", p=128),
                   "w", max_dma_last_dim=4096)

    def dma_wet(e):
        return DMA("gp", wet_sl[e % 2].rearrange("p (dc o) -> p dc o", o=D),
                   wet_h[e].rearrange("(dc p) o -> p dc o", p=128), "w", max_dma_last_dim=4096)

    I("ve", lambda E: E.memset(consts[:, 0:1], 1e-5))
    I("ve", lambda E: E.memset(consts[:, 1:2], -0.5))
    I("ve", lambda E: E.memset(consts[:, 2:3], -1.0))
    I("ve", lambda E: E.memset(consts[:, 3:4], 1.0 / 1024.0))
    I("ve", lambda E: E.memset(consts[:, 8:9], 0.0))
    I("ve", lambda E: E.memset(consts[:, 9:10], float(CAP + 1)))
    I("ve", lambda E: E.memset(consts[:, 10:11], BIG))
    I("ve", lambda E: E.memset(ones_row[:, :], 1.0))
    I("ve", lambda E: E.memset(ones8[:, :], 1.0))

    zview = outst[:, :].bitcast(BF16)[:, 0:ROWW]
    I("ve", lambda E: E.memset(zview, 0.0))
    c_zero_t = VSYNC()
    W("sy", "dve", c_zero_t)
    for zc in range(NSLOT // 128):
        DMA("sy", perm_h[zc * 128 : (zc + 1) * 128, :], zview, "x")
    c_permzero = cnt["x"]
    c_ident = DMA("sy", ident[:, :], idb_h[:, :], "x")
    c_gw = DMA("sy", gwf[:, :].rearrange("p (dc e) -> p dc e", e=8),
               wg_h.rearrange("(dc p) e -> p dc e", p=128), "x")
    c_wrtf = DMA("sy", wrtf[:, :].rearrange("p (dc e) -> p dc e", e=8),
                 wrt_h.rearrange("(dc p) e -> p dc e", p=128), "x")
    c_eoff = DMA("sy", eoff_sb[:, :], eoff_h[:, :], "x")
    DMA("sy", bgbrst[:, 0:8], bg_h.unsqueeze(0), "x")
    c_bgbrst = DMA("sy", bgbrst[:, 8:16], br_h.unsqueeze(0), "x")

    c_b2 = DMA("gp", bias_stack[0:8, :], b2_h[:, :], "w")
    c_be = DMA("gp", bias_stack[8:16, :], be_h[:, :], "w")
    c_b1l = DMA("gp", b1st[:, :].rearrange("hc (e p) -> hc e p", p=128),
                b1_h.rearrange("e (hc p) -> hc e p", p=128), "w")
    W("te", "w", c_b1l)
    W("te", "x", c_ident)
    for e in range(NEXP):
        I("te", lambda E, e=e: E.matmul(psD[:, e * 16 : e * 16 + 16],
                                        b1st[:, e * 128 : (e + 1) * 128],
                                        ident[0:16, 0:16], start=(e == 0), stop=(e == NEXP - 1),
                                        skip_group_check=True))
    c_b1tr = INC("te", "pe")
    W("sc", "pe", c_b1tr)
    I("sc", lambda E: E.copy(b1sb[:, :], psD[:, 0:128]))
    c_b1_done = INC("sc", "act")

    W("ve", "x", max(c_gw, c_bgbrst, c_wrtf))
    gwf_v = gwf[:, :].rearrange("p (dc e) -> p dc e", e=8)
    wrtf_v = wrtf[:, :].rearrange("p (dc e) -> p dc e", e=8)
    gate_v = gate_w[:, :].rearrange("p (dc c) -> p dc c", c=32)
    I("ve", lambda E: E.tensor_copy(gate_v[:, :, 0:8], gwf_v))
    I("ve", lambda E: E.tensor_copy(gate_v[:, :, 8:16], wrtf_v))
    VSYNC()
    I("ve", lambda E: E.tensor_tensor(gate_v[:, :, 16:24], gwf_v, gate_v[:, :, 0:8], op=OP.subtract))
    I("ve", lambda E: E.tensor_tensor(gate_v[:, :, 24:32], wrtf_v, gate_v[:, :, 8:16], op=OP.subtract))
    I("ve", lambda E: E.tensor_copy(bgbr[:, :], bgbrst[:, :]))
    VSYNC()
    c_init_dve = cnt["dve"]

    xst_last = [0, 0]
    pe_attn_done = 0
    st_out = [0] * CHUNKS
    ps_rel = {"A": ("dve", 0), "B": ("dve", 0), "C": ("dve", 0), "D": ("dve", 0)}
    act_xn_evac = [0] * CHUNKS

    def ps_use(psname):
        W("te", *ps_rel[psname])

    def ps_release(psname, semname, count):
        ps_rel[psname] = (semname, count)

    for s in range(SUPS):
        ph1 = [dict() for _ in range(CHUNKS)]

        def ph1_ln(ci, s=s):
            lnw = lnw_b[ci % 2]; gw48 = gw48_b[ci % 2]; rwbuf = rwbuf_b[ci % 2]
            xnf32 = xnf32_b[ci % 2]; xlot = xlot_b[ci % 2]
            g_ci = s * CHUNKS + ci
            W("sy", "dve", xst_last[ci % 2])
            c_x = DMA("sy", xst[:, (ci % 2) * D : (ci % 2 + 1) * D],
                      x_h[g_ci * 128 : (g_ci + 1) * 128, :], "x")
            xs = xst[:, (ci % 2) * D : (ci % 2 + 1) * D]
            W("ve", "x", c_x)
            if ci == 0:
                if s > 0:
                    W("ve", "pe", pe_attn_done)
                else:
                    W("ve", "x", c_permzero)
                    W("ve", "pe", c_b1tr)
            xh = xn_tok[:, ci * ROWW : ci * ROWW + D]
            I("ve", lambda E, xs=xs: E.bn_stats(lnw[:, 0:6], xs[:, 0:512]))
            I("ve", lambda E, xs=xs: E.bn_stats(lnw[:, 6:12], xs[:, 512:1024]))
            SP()
            SP()
            I("ve", lambda E: E.bn_aggr(lnw[:, 2:4], lnw[:, 0:12]))
            c_stats = INC("ve", "dve")
            W("sc", "dve", c_stats)
            I("sc", lambda E: E.activation(lnw[:, 5:6], lnw[:, 3:4], AF.Ln, bias=consts[:, 0:1]))
            I("sc", lambda E: E.activation(lnw[:, 6:7], lnw[:, 5:6], AF.Exp, scale=consts[:, 1:2]))
            c_rstd = INC("sc", "act")
            W("ve", "act", c_rstd)
            I("ve", lambda E: E.tensor_scalar(lnw[:, 7:8], lnw[:, 2:3], lnw[:, 6:7], consts[:, 2:3],
                                              op0=OP.mult, op1=OP.mult))
            SP()
            I("ve", lambda E, xs=xs, xh=xh: E.tensor_scalar(xh, xs, lnw[:, 6:7], lnw[:, 7:8],
                                                            op0=OP.mult, op1=OP.add))
            I("ve", lambda E, xs=xs: E.tensor_scalar(xnf32[:, :], xs, lnw[:, 6:7], lnw[:, 7:8],
                                                     op0=OP.mult, op1=OP.add))
            SP()
            I("ve", lambda E, xh=xh: E.tensor_tensor(xlot[:, :], xnf32[:, :], xh, op=OP.subtract))
            c_xn_tok = INC("ve", "dve")
            xst_last[ci % 2] = c_xn_tok
            ph1[ci]["xn_tok"] = c_xn_tok

        def ph1_mid(ci, s=s):
            lnw = lnw_b[ci % 2]; gw48 = gw48_b[ci % 2]; rwbuf = rwbuf_b[ci % 2]
            xnf32 = xnf32_b[ci % 2]; xlot = xlot_b[ci % 2]
            xh = xn_tok[:, ci * ROWW : ci * ROWW + D]
            W("te", "dve", ph1[ci]["xn_tok"])
            if s == 0 and ci == 0:
                W("te", "dve", c_init_dve)
                W("te", "act", c_b1_done)
            ps_use("C")
            for dc in range(DC):
                I("te", lambda E, dc=dc, xh=xh: E.matmul(
                    psC[:, dc * 128 : (dc + 1) * 128], xh[:, dc * 128 : (dc + 1) * 128],
                    ident[:, :], start=(dc % 4 == 0), stop=(dc % 4 == 3), skip_group_check=True))
            c_trhi = INC("te", "pe")
            ps_use("D")
            for dc in range(DC):
                I("te", lambda E, dc=dc: E.matmul(
                    psD[:, dc * 128 : (dc + 1) * 128], xlot[:, dc * 128 : (dc + 1) * 128],
                    ident[:, :], start=(dc % 4 == 0), stop=(dc % 4 == 3), skip_group_check=True))
            c_trlo = INC("te", "pe")
            W("sc", "pe", c_trhi)
            xn_dst = xn_fm.rearrange("p (dc n) -> p dc n", n=NSUP)[:, :, ci * 128 : (ci + 1) * 128]
            I("sc", lambda E, xn_dst=xn_dst: E.copy(xn_dst, psC[:, :].rearrange("p (dc n) -> p dc n", n=128)))
            c_xnhi_evac = INC("sc", "act")
            act_xn_evac[ci] = c_xnhi_evac
            ps_release("C", "act", c_xnhi_evac)
            W("sc", "pe", c_trlo)
            I("sc", lambda E: E.copy(xlofm[:, :], psD[:, :]))
            c_xnlo_evac = INC("sc", "act")
            ps_release("D", "act", c_xnlo_evac)
            W("te", "act", c_xnlo_evac)
            ps_use("C")
            for dc in range(DC):
                I("te", lambda E, dc=dc, ci=ci: E.matmul(
                    psC[:, 0:32],
                    xn_fm[:, dc * NSUP + ci * 128 : dc * NSUP + (ci + 1) * 128],
                    gate_w[:, dc * 32 : dc * 32 + 32],
                    start=(dc == 0), stop=False, skip_group_check=True))
            for dc in range(DC):
                I("te", lambda E, dc=dc: E.matmul(
                    psC[:, 32:48], xlofm[:, dc * 128 : (dc + 1) * 128],
                    gate_w[:, dc * 32 : dc * 32 + 16],
                    start=False, stop=False, skip_group_check=True))
            I("te", lambda E: E.matmul(psC[:, 0:16], ones_row[:, :], bgbr[:, :],
                                       start=False, stop=True, skip_group_check=True))
            c_gate = INC("te", "pe")
            W("ve", "pe", c_gate)
            I("ve", lambda E: E.tensor_copy(gw48[:, :], psC[:, 0:48]))
            c_gate_evac = INC("ve", "dve")
            ps_release("C", "dve", c_gate_evac)
            SP()
            I("ve", lambda E: E.tensor_tensor(lnw[:, 16:24], gw48[:, 16:24], gw48[:, 32:40], op=OP.add))
            I("ve", lambda E: E.tensor_tensor(lnw[:, 24:32], gw48[:, 24:32], gw48[:, 40:48], op=OP.add))
            I("ve", lambda E: E.tensor_tensor(lnw[:, 32:40], lnw[:, 16:24], gw48[:, 0:8], op=OP.add))
            I("ve", lambda E: E.tensor_tensor(lnw[:, 40:48], lnw[:, 24:32], gw48[:, 8:16], op=OP.add))
            I("ve", lambda E: E.reduce_max(lnw[:, 48:49], lnw[:, 32:40], axis=AX.X, negate=True))
            I("ve", lambda E: E.reduce_max(lnw[:, 49:50], lnw[:, 40:48], axis=AX.X, negate=True))
            I("ve", lambda E: E.reduce_max(lnw[:, 50:51], lnw[:, 32:40], axis=AX.X))
            c_glog = INC("ve", "dve")
            W("sc", "dve", c_glog)
            I("sc", lambda E: E.activation(lnw[:, 56:64], lnw[:, 32:40], AF.Exp, bias=lnw[:, 48:49]))
            I("sc", lambda E: E.activation(lnw[:, 64:72], lnw[:, 40:48], AF.Exp, bias=lnw[:, 49:50]))
            c_exps = INC("sc", "act")
            W("ve", "act", c_exps)
            I("ve", lambda E: E.reduce_sum(lnw[:, 51:52], lnw[:, 56:64], axis=AX.X))
            I("ve", lambda E: E.reduce_sum(lnw[:, 52:53], lnw[:, 64:72], axis=AX.X))
            I("ve", lambda E: E.reciprocal(lnw[:, 53:54], lnw[:, 51:52]))
            I("ve", lambda E: E.reciprocal(lnw[:, 54:55], lnw[:, 52:53]))
            I("ve", lambda E: E.tensor_scalar(lnw[:, 72:80], lnw[:, 32:40], lnw[:, 50:51], None,
                                              op0=OP.is_equal))
            I("ve", lambda E: E.tensor_scalar(rwbuf[:, 0:8], lnw[:, 64:72], lnw[:, 54:55], None,
                                              op0=OP.mult))
            I("ve", lambda E, ci=ci: E.tensor_scalar(s_sb[:, ci * 16 : ci * 16 + 8], lnw[:, 72:80],
                                                     lnw[:, 53:54], None, op0=OP.mult))
            I("ve", lambda E: E.reduce_max(rwbuf[:, 8:9], rwbuf[:, 0:8], axis=AX.X))
            tw_dst = xn_tok[:, ci * ROWW + D : (ci + 1) * ROWW]
            I("ve", lambda E, tw_dst=tw_dst: E.tensor_scalar(tw_dst, lnw[:, 56:64], consts[:, 8:9],
                                                             lnw[:, 53:54], op0=OP.mult, op1=OP.add))
            I("ve", lambda E: E.tensor_scalar(rwbuf[:, 16:24], rwbuf[:, 0:8], rwbuf[:, 8:9], None,
                                              op0=OP.is_lt))
            SP()
            I("ve", lambda E: E.tensor_tensor(rwbuf[:, 24:32], rwbuf[:, 0:8], rwbuf[:, 16:24], op=OP.mult))
            SP()
            I("ve", lambda E: E.reduce_max(rwbuf[:, 9:10], rwbuf[:, 24:32], axis=AX.X))
            SP()
            I("ve", lambda E: E.tensor_scalar(rwbuf[:, 32:40], rwbuf[:, 0:8], rwbuf[:, 9:10], None,
                                              op0=OP.is_ge))
            SP()
            I("ve", lambda E, ci=ci: E.tensor_tensor(s_sb[:, ci * 16 + 8 : ci * 16 + 16], rwbuf[:, 0:8],
                                                     rwbuf[:, 32:40], op=OP.mult))
            c_sr = INC("ve", "dve")
            ph1[ci]["sr"] = c_sr

        def ph1_tail(ci, s=s):
            W("te", "dve", ph1[ci]["sr"])
            ps_use("C")
            I("te", lambda E, ci=ci: E.matmul(psC[0:16, 512:640], s_sb[:, ci * 16 : ci * 16 + 16],
                                              ident[:, :], start=True, stop=True))
            c_srT = INC("te", "pe")
            W("sc", "pe", c_srT)
            I("sc", lambda E, ci=ci: E.copy(srT[:, ci * 128 : (ci + 1) * 128], psC[0:16, 512:640]))
            c_srT_evac = INC("sc", "act")
            ps_release("C", "act", c_srT_evac)
            W("te", "act", c_srT_evac)
            if s == 0 and ci == 0:
                W("te", "w", max(c_b2, c_be))
            ps_use("D")
            I("te", lambda E, ci=ci: E.matmul(psD[:, 0:512], srT[:, ci * 128 : (ci + 1) * 128],
                                              bias_stack[:, 0:512], start=True, stop=True))
            I("te", lambda E, ci=ci: E.matmul(psD[:, 512:1024], srT[:, ci * 128 : (ci + 1) * 128],
                                              bias_stack[:, 512:1024], start=True, stop=True))
            c_bias = INC("te", "pe")
            W("ve", "pe", c_bias)
            W("ve", "st", st_out[ci])
            I("ve", lambda E, ci=ci: E.tensor_copy(acc[:, ci * D : (ci + 1) * D], psD[:, :]))
            c_accinit = INC("ve", "dve")
            ps_release("D", "dve", c_accinit)

        dve_ffn = {}
        pe_ffn_done = [0] * NEXP

        def ffn_chunk(e, ci):
            ps3, nm3 = (psA, "A") if (ci + e) % 2 == 0 else (psB, "B")
            ps_use(nm3)
            for dc in range(DC):
                I("te", lambda E, dc=dc, ci=ci, e=e, ps3=ps3: E.matmul(
                    ps3[:, 0:512],
                    xn_fm[:, dc * NSUP + ci * 128 : dc * NSUP + (ci + 1) * 128],
                    wet_sl[e % 2][:, dc * D : dc * D + 512],
                    start=(dc == 0), stop=(dc == DC - 1)))
                I("te", lambda E, dc=dc, ci=ci, e=e, ps3=ps3: E.matmul(
                    ps3[:, 512:1024],
                    xn_fm[:, dc * NSUP + ci * 128 : dc * NSUP + (ci + 1) * 128],
                    wet_sl[e % 2][:, dc * D + 512 : dc * D + 1024],
                    start=(dc == 0), stop=(dc == DC - 1)))
            c_f = INC("te", "pe")
            W("ve", "pe", c_f)
            I("ve", lambda E, ci=ci, e=e, ps3=ps3: E.scalar_tensor_tensor(
                acc[:, ci * D : (ci + 1) * D], ps3[:, :],
                s_sb[:, ci * 16 + 8 + e : ci * 16 + 8 + e + 1],
                acc[:, ci * D : (ci + 1) * D], op0=OP.mult, op1=OP.add))
            dve_ffn[(e, ci)] = INC("ve", "dve")
            ps_release(nm3, "dve", dve_ffn[(e, ci)])

        c_wet = [0] * NEXP
        if s > 0:
            W("gp", "pe", pe_attn_done)
        c_wet[0] = dma_wet(0)
        c_wet[1] = dma_wet(1)
        for cp in range(0, CHUNKS, 2):
            ph1_ln(cp)
            ph1_ln(cp + 1)
            ph1_mid(cp)
            ph1_mid(cp + 1)
            ph1_tail(cp)
            ph1_tail(cp + 1)
            if cp == 0:
                W("te", "w", c_wet[0])
                W("te", "w", c_wet[1])
            W("te", "act", act_xn_evac[cp])
            ffn_chunk(0, cp)
            ffn_chunk(1, cp)
            W("te", "act", act_xn_evac[cp + 1])
            ffn_chunk(0, cp + 1)
            ffn_chunk(1, cp + 1)
        pe_ffn_done[0] = pe_ffn_done[1] = cnt["pe"]
        W("gp", "pe", cnt["pe"])
        c_wet[2] = dma_wet(2)
        c_wet[3] = dma_wet(3)
        I("ve", lambda E: E.tensor_scalar(maskb[:, :], srT[0:8, :], consts[0:8, 8:9], None,
                                          op0=OP.is_gt))
        VSYNC()
        I("ve", lambda E: E.tensor_tensor_scan(pfx[:, :], maskb[:, :], maskb[:, :],
                                               initial=0.0, op0=OP.add, op1=OP.max))
        VSYNC()
        W("ve", *ps_rel["C"])
        W("ve", *ps_rel["D"])
        tv = [psC[0:8, :], psD[0:8, :]]
        for h in range(2):
            cs = slice(h * 1024, (h + 1) * 1024)
            I("ve", lambda E, h=h, cs=cs: E.tensor_scalar(tv[h], pfx[:, cs], consts[0:8, 9:10],
                                                          None, op0=OP.is_ge))
        VSYNC()
        for h in range(2):
            cs = slice(h * 1024, (h + 1) * 1024)
            I("ve", lambda E, h=h, cs=cs: E.tensor_tensor(tv[h], tv[h], maskb[:, cs], op=OP.mult))
        VSYNC()
        I("ve", lambda E: E.scalar_tensor_tensor(pfx[:, :], pfx[:, :], eoff_sb[:, 0:1],
                                                 maskb[:, :], op0=OP.add, op1=OP.mult))
        VSYNC()
        for h in range(2):
            cs = slice(h * 1024, (h + 1) * 1024)
            I("ve", lambda E, h=h, cs=cs: E.scalar_tensor_tensor(pfx[:, cs], tv[h],
                                                                 consts[0:8, 10:11],
                                                                 pfx[:, cs], op0=OP.mult, op1=OP.add))
        c_slotraw = VSYNC()

        def emit_slots_and_scatter():
            nonlocal c_slotu_out, c_scatter_out
            W("te", "dve", c_slotraw)
            W("te", "x", c_eoff)
            ps_use("D")
            for ci in range(CHUNKS):
                I("te", lambda E, ci=ci: E.matmul(psD[:, ci : ci + 1], pfx[:, ci * 128 : (ci + 1) * 128],
                                                  ones8[:, :], start=(ci == 0), stop=(ci == CHUNKS - 1)))
            I("te", lambda E: E.matmul(psD[:, 512:1024], pfx[:, 0:128], pfx[:, 0:512],
                                       start=True, stop=True, skip_group_check=True))
            c_slotmm = INC("te", "pe")
            W("ve", "pe", c_slotmm)
            I("ve", lambda E: E.tensor_copy(slotf[:, :], psD[:, 0:CHUNKS]))
            VSYNC()
            I("ve", lambda E: E.tensor_copy(slotu[:, :], slotf[:, :]))
            c_slotu = VSYNC()
            ps_release("D", "dve", c_slotu)
            ps_release("C", "dve", c_slotu)

            if debug and s == 0:
                W("sy", "dve", c_slotu)
                DMA("sy", dbg_slot_h, slotu[:, :], "st")
                DMA("sy", dbg_s_h, s_sb[:, :], "st")
            W("gp", "dve", c_slotu)
            W("gp", "x", c_permzero)
            for ci in range(CHUNKS):
                IDMA("gp", perm_h[:, :], bass.IndirectOffsetOnAxis(ap=slotu[:, ci : ci + 1], axis=0),
                     xn_tok[:, ci * ROWW : (ci + 1) * ROWW], None, "w",
                     bounds_check=NSLOT - 1, oob_is_err=False)
            c_scatter = cnt["w"]

            c_slotu_out = c_slotu
            c_scatter_out = c_scatter
        c_slotu_out = 0
        c_scatter_out = 0
        for e in range(2, NEXP):
            W("te", "w", c_wet[e])
            for ci in range(CHUNKS):
                ffn_chunk(e, ci)
            pe_ffn_done[e] = cnt["pe"]
            if e + 2 < NEXP:
                W("gp", "pe", pe_ffn_done[e])
                c_wet[e + 2] = dma_wet(e + 2)
            if e == 2:
                emit_slots_and_scatter()

        c_w1 = {}
        W("gp", "w", c_scatter_out)
        W("gp", "pe", pe_ffn_done[4])
        c_w1[(0, 0)] = dma_w1q(0, 0, 0)
        c_w1[(0, 1)] = dma_w1q(0, 1, 1)
        W("gp", "pe", pe_ffn_done[6])
        c_w2 = {0: dma_w2(0)}
        c_pch = {}
        W("sy", "pe", pe_ffn_done[NEXP - 1])
        W("sy", "w", c_scatter_out)
        for j in range(6):
            c_pch[(0, j)] = DMA("sy", pchunk[j], perm_h[j * 128 : (j + 1) * 128, :], "x")

        act_relu = {}
        act_pfm = {}
        st_stage = [0, 0]
        dve_tw = [0]

        for e in range(NEXP):
            for j in range(SEGCH):
                g_abs = e * SEGCH + j
                pch = pchunk[g_abs % 6]
                W("te", "x", c_pch[(e, j)])
                psT, nmT = (psA, "A") if j % 2 == 0 else (psB, "B")
                ps_use(nmT)
                for dc in range(DC):
                    I("te", lambda E, dc=dc, pch=pch, psT=psT: E.matmul(
                        psT[:, dc * 128 : (dc + 1) * 128], pch[:, dc * 128 : (dc + 1) * 128],
                        ident[:, :], start=(dc % 4 == 0), stop=(dc % 4 == 3),
                        skip_group_check=True))
                c_tr = INC("te", "pe")
                W("sc", "pe", c_tr)
                pdst = pfm.rearrange("p (dc n) -> p dc n", n=CAP)[:, :, j * 128 : (j + 1) * 128]
                I("sc", lambda E, pdst=pdst, psT=psT: E.copy(
                    pdst, psT[:, :].rearrange("p (dc n) -> p dc n", n=128)))
                act_pfm[(e, j)] = INC("sc", "act")
                ps_release(nmT, "act", act_pfm[(e, j)])
                W("ve", "x", c_pch[(e, j)])
                I("ve", lambda E, j=j, pch=pch: E.tensor_copy(topw_sc[:, j : j + 1],
                                                              pch[:, D : D + 1]))
                dve_tw[0] = INC("ve", "dve")
                gn = g_abs + 6
                ne, nj = gn // SEGCH, gn % SEGCH
                if ne < NEXP:
                    W("sy", "dve", dve_tw[0])
                    W("sy", "act", act_pfm[(e, j)])
                    c_pch[(ne, nj)] = DMA("sy", pchunk[g_abs % 6],
                                          perm_h[gn * 128 : (gn + 1) * 128, :], "x")
            c_m1 = 0
            for q in range(4):
                buf = q % 2
                W("te", "w", c_w1[(e, q)])
                if q == 0:
                    W("te", "act", act_pfm[(e, SEGCH - 1)])
                for hq in range(4):
                    hc = q * 4 + hq
                    ps1, nm1 = (psA, "A") if hc % 2 == 0 else (psB, "B")
                    ps_use(nm1)
                    for dc in range(DC):
                        I("te", lambda E, dc=dc, hq=hq, buf=buf, ps1=ps1: E.matmul(
                            ps1[:, 0:512],
                            w1q[buf][:, dc * 512 + hq * 128 : dc * 512 + (hq + 1) * 128],
                            pfm[:, dc * CAP : dc * CAP + 512],
                            start=(dc == 0), stop=(dc == DC - 1)))
                        I("te", lambda E, dc=dc, hq=hq, buf=buf, ps1=ps1: E.matmul(
                            ps1[:, 512:768],
                            w1q[buf][:, dc * 512 + hq * 128 : dc * 512 + (hq + 1) * 128],
                            pfm[:, dc * CAP + 512 : dc * CAP + 768],
                            start=(dc == 0), stop=(dc == DC - 1)))
                    c_m1 = INC("te", "pe")
                    W("sc", "pe", c_m1)
                    I("sc", lambda E, e=e, hc=hc, ps1=ps1: E.activation(
                        hidseg[:, hc * CAP : (hc + 1) * CAP], ps1[:, 0:CAP], AF.Relu,
                        bias=b1sb[:, e * HC + hc : e * HC + hc + 1]))
                    act_relu[(e, hc)] = INC("sc", "act")
                    ps_release(nm1, "act", act_relu[(e, hc)])
                nq_e, nq = (e, q + 2) if q + 2 < 4 else (e + 1, q - 2)
                if nq_e < NEXP:
                    W("gp", "pe", c_m1)
                    c_w1[(nq_e, nq)] = dma_w1q(nq_e, nq, buf)
            c_m2 = 0
            for j in range(SEGCH):
                ps2, nm2 = (psC, "C") if j % 2 == 0 else (psD, "D")
                if j == 0:
                    W("te", "w", c_w2[e])
                ps_use(nm2)
                for hc in range(HC):
                    if j == 0:
                        W("te", "act", act_relu[(e, hc)])
                    I("te", lambda E, hc=hc, j=j, ps2=ps2: E.matmul(
                        ps2[:, 0:512],
                        hidseg[:, hc * CAP + j * 128 : hc * CAP + (j + 1) * 128],
                        w2sb[:, hc * D : hc * D + 512],
                        start=(hc == 0), stop=(hc == HC - 1)))
                    I("te", lambda E, hc=hc, j=j, ps2=ps2: E.matmul(
                        ps2[:, 512:1024],
                        hidseg[:, hc * CAP + j * 128 : hc * CAP + (j + 1) * 128],
                        w2sb[:, hc * D + 512 : hc * D + 1024],
                        start=(hc == 0), stop=(hc == HC - 1)))
                c_m2 = INC("te", "pe")
                stg = outst[:, (j % 2) * D : (j % 2 + 1) * D]
                W("ve", "pe", c_m2)
                W("ve", "st", st_stage[j % 2])
                I("ve", lambda E, stg=stg, j=j, ps2=ps2: E.tensor_scalar(
                    stg, ps2[:, :], topw_sc[:, j : j + 1], None, op0=OP.mult))
                c_ev = INC("ve", "dve")
                ps_release(nm2, "dve", c_ev)
                W("sc", "dve", c_ev)
                st_stage[j % 2] = DMA("sc",
                                      aperm_h[e * CAP + j * 128 : e * CAP + (j + 1) * 128, :],
                                      stg, "st")
            if e + 1 < NEXP:
                W("gp", "pe", c_m2)
                c_w2[e + 1] = dma_w2(e + 1)
        pe_attn_done = cnt["pe"]
        c_aperm_done = cnt["st"]

        if debug and s == 0:
            W("sy", "st", c_aperm_done)
            DMA("sy", dbg_ap_h, aperm_h[:, :], "st")
        dve_gb = [0, 0]
        for ci in range(CHUNKS):
            g_ci = s * CHUNKS + ci
            gb = outst[:, (ci % 2) * D : (ci % 2 + 1) * D].bitcast(BF16).bitcast(F32)
            if ci == 0:
                W("gp", "st", c_aperm_done)
            W("gp", "dve", dve_gb[ci % 2])
            c_g = IDMA("gp", gb, None, aperm_h[:, :],
                       bass.IndirectOffsetOnAxis(ap=slotu[:, ci : ci + 1], axis=0), "w",
                       bounds_check=NSLOT - 1, oob_is_err=False)
            W("sy", "dve", xst_last[ci % 2])
            c_x = DMA("sy", xst[:, (ci % 2) * D : (ci % 2 + 1) * D],
                      x_h[g_ci * 128 : (g_ci + 1) * 128, :], "x")
            xs = xst[:, (ci % 2) * D : (ci % 2 + 1) * D]
            W("ve", "w", c_g)
            W("ve", "x", c_x)
            I("ve", lambda E, ci=ci, gb=gb: E.tensor_tensor(
                acc[:, ci * D : (ci + 1) * D], acc[:, ci * D : (ci + 1) * D],
                gb, op=OP.add))
            dve_gb[ci % 2] = INC("ve", "dve")
            SP()
            I("ve", lambda E, xs=xs, ci=ci: E.tensor_tensor(
                acc[:, ci * D : (ci + 1) * D], xs, acc[:, ci * D : (ci + 1) * D], op=OP.add))
            c_res = INC("ve", "dve")
            xst_last[ci % 2] = c_res
            W("sy", "dve", c_res)
            st_out[ci] = DMA("sy", out_h[g_ci * 128 : (g_ci + 1) * 128, :],
                             acc[:, ci * D : (ci + 1) * D], "st")

    W("sy", "st", cnt["st"])

    psem = {k: EN(nc.semaphore("psem_" + k)) for k in ("ve", "sc", "gp")} if paranoid else None
    pcnt = {k: 0 for k in ("ve", "sc", "gp")}

    def replay(engobj, key):
        strict = paranoid and key in ("ve", "sc", "gp")
        breg = [None]
        for item in ops[key]:
            kind = item[0]
            if kind == "call":
                inst = item[1](engobj)
                if strict and inst is not None:
                    pcnt[key] += 1
                    inst.then_inc(psem[key], 1)
                    engobj.wait_ge(psem[key], pcnt[key])
            elif kind == "inccall":
                inst = item[1](engobj)
                inst.then_inc(item[2], item[3])
                if strict:
                    engobj.wait_ge(item[2], item[4])
            elif kind == "wait":
                engobj.wait_ge(item[1], item[2])
            elif kind == "dma":
                engobj.dma_start(item[1], item[2], **item[4]).then_inc(item[3], 16)
            elif kind == "idma":
                kw = dict(item[6])
                if "bounds_check" in kw:
                    if breg[0] is None:
                        breg[0] = engobj.to_reg(kw["bounds_check"])
                    kw["bounds_check"] = breg[0]
                engobj.indirect_dma_start(item[1], item[2], item[3], item[4],
                                          **kw).then_inc(item[5], 16)

    block = EN(nc.Block())

    @block.tensor
    def _(te):
        replay(te, "te")

    @block.vector
    def _(ve):
        replay(ve, "ve")

    @block.scalar
    def _(sc):
        replay(sc, "sc")

    @block.gpsimd
    def _(gp):
        replay(gp, "gp")

    @block.sync
    def _(sy):
        replay(sy, "sy")

    ctx.close()
    return nc


_prebuilt = None


def kernel(L0_attn_x, L0_attn_Wg, L0_attn_bg, L0_attn_W1, L0_attn_b1, L0_attn_W2,
           L0_attn_b2, L0_ffn_W_experts, L0_ffn_b_experts, L0_ffn_W_router, L0_ffn_b_router):
    global _prebuilt
    x = np.ascontiguousarray(np.asarray(L0_attn_x, dtype=np.float32).reshape(NTOK, D))
    shards = np.split(x, NCORE, axis=0)
    common = dict(
        wg=np.ascontiguousarray(L0_attn_Wg, dtype=np.float32),
        bg=np.ascontiguousarray(L0_attn_bg, dtype=np.float32),
        w1=np.ascontiguousarray(L0_attn_W1, dtype=np.float32),
        b1=np.ascontiguousarray(L0_attn_b1, dtype=np.float32),
        w2=np.ascontiguousarray(L0_attn_W2, dtype=np.float32),
        b2=np.ascontiguousarray(L0_attn_b2, dtype=np.float32),
        wet=np.ascontiguousarray(np.asarray(L0_ffn_W_experts, dtype=np.float32).transpose(0, 2, 1)),
        be=np.ascontiguousarray(L0_ffn_b_experts, dtype=np.float32),
        wrt=np.ascontiguousarray(np.asarray(L0_ffn_W_router, dtype=np.float32).T),
        br=np.ascontiguousarray(L0_ffn_b_router, dtype=np.float32),
        idb=np.eye(128, dtype=ml_dtypes.bfloat16),
        eoffs=(np.arange(NEXP, dtype=np.float32) * CAP - 1.0).reshape(NEXP, 1),
    )
    if _prebuilt is None:
        _prebuilt = build_program()
    in_maps = [dict(common, x=sh) for sh in shards]
    res = run_bass_kernel_spmd(_prebuilt, in_maps, core_ids=list(range(NCORE)))
    out = np.concatenate([r["out"] for r in res.results], axis=0)
    return out.reshape(2, 16384, D)


if __name__ == "__main__":
    build_program()
    print("built ok")


# version 26
# speedup vs baseline: 1.0298x; 1.0298x over previous
import numpy as np
import ml_dtypes
import concourse.bass as bass
import concourse.mybir as mybir
from concourse.bass_utils import run_bass_kernel_spmd
from contextlib import ExitStack

F32 = mybir.dt.float32
BF16 = mybir.dt.bfloat16
U32 = mybir.dt.uint32
AF = mybir.ActivationFunctionType
AX = mybir.AxisListType
OP = mybir.AluOpType

D = 1024
H = 2048
NEXP = 8
NCORE = 8
NTOK = 2 * 16384
NC = NTOK // NCORE
NSUP = 2048
SUPS = NC // NSUP
CHUNKS = NSUP // 128
DC = D // 128
HC = H // 128
CAP = 768
SEGCH = CAP // 128
NSLOT = NEXP * CAP
ROWW = D + 8
BIG = float(1 << 28)


def build_program(debug=False, paranoid=None):
    import os
    if paranoid is None:
        paranoid = bool(os.environ.get("BASS_SIM_FP64"))
    nc = bass.Bass("TRN2", target_bir_lowering=False)

    x_h = nc.dram_tensor("x", [NC, D], F32, kind="ExternalInput").ap()
    wg_h = nc.dram_tensor("wg", [D, NEXP], F32, kind="ExternalInput").ap()
    bg_h = nc.dram_tensor("bg", [NEXP], F32, kind="ExternalInput").ap()
    w1_h = nc.dram_tensor("w1", [NEXP, D, H], F32, kind="ExternalInput").ap()
    b1_h = nc.dram_tensor("b1", [NEXP, H], F32, kind="ExternalInput").ap()
    w2_h = nc.dram_tensor("w2", [NEXP, H, D], F32, kind="ExternalInput").ap()
    b2_h = nc.dram_tensor("b2", [NEXP, D], F32, kind="ExternalInput").ap()
    wet_h = nc.dram_tensor("wet", [NEXP, D, D], F32, kind="ExternalInput").ap()
    be_h = nc.dram_tensor("be", [NEXP, D], F32, kind="ExternalInput").ap()
    wrt_h = nc.dram_tensor("wrt", [D, NEXP], F32, kind="ExternalInput").ap()
    br_h = nc.dram_tensor("br", [NEXP], F32, kind="ExternalInput").ap()
    idb_h = nc.dram_tensor("idb", [128, 128], BF16, kind="ExternalInput").ap()
    eoff_h = nc.dram_tensor("eoffs", [NEXP, 1], F32, kind="ExternalInput").ap()
    out_h = nc.dram_tensor("out", [NC, D], F32, kind="ExternalOutput").ap()
    scratch_kind = "ExternalOutput" if debug else "Internal"
    perm_h = nc.dram_tensor("perm_x", [NSLOT, ROWW], BF16, kind=scratch_kind).ap()
    aperm_h = nc.dram_tensor("attn_perm", [NSLOT, D], F32, kind=scratch_kind).ap()
    if debug:
        dbg_slot_h = nc.dram_tensor("dbg_slot", [128, CHUNKS], U32, kind="ExternalOutput").ap()
        dbg_s_h = nc.dram_tensor("dbg_s", [128, CHUNKS * 16], BF16, kind="ExternalOutput").ap()
        dbg_ap_h = nc.dram_tensor("dbg_ap", [NSLOT, D], F32, kind="ExternalOutput").ap()

    ctx = ExitStack()
    EN = ctx.enter_context
    acc = EN(nc.sbuf_tensor("acc", [128, CHUNKS * D], F32))
    R1 = EN(nc.sbuf_tensor("R1", [128, 36864], BF16))
    xn_tok = R1[:, 0:CHUNKS * ROWW]
    wet_sl = [R1[:, 16512:24704], R1[:, 24704:32896]]
    w1q = [R1[:, 0:4096], R1[:, 4096:8192]]
    w2sb = R1[:, 8192:24576]
    hidseg = R1[:, 24576:36864]
    R2 = EN(nc.sbuf_tensor("R2", [128, 16384], BF16))
    xn_fm = R2[:, 0:16384]
    pfm = R2[:, 0:6144]
    pchunk = [R2[:, 6144 + k * ROWW : 6144 + (k + 1) * ROWW] for k in range(6)]
    gbuf = [R2[:, 8256:10304].bitcast(F32), R2[:, 10304:12352].bitcast(F32),
            R2[:, 12352:14400].bitcast(F32)]

    s_sb = EN(nc.sbuf_tensor("s_sb", [128, CHUNKS * 16], BF16))
    srT = EN(nc.sbuf_tensor("srT", [16, NSUP], BF16))
    maskb = EN(nc.sbuf_tensor("maskb", [8, NSUP], BF16))
    pfx = EN(nc.sbuf_tensor("pfx", [8, NSUP], F32))
    b1sb = EN(nc.sbuf_tensor("b1sb", [128, NEXP * HC], F32))
    bias_stack = EN(nc.sbuf_tensor("bias_stack", [16, D], BF16))
    gate_w = EN(nc.sbuf_tensor("gate_w", [128, DC * 32], BF16))
    ident = EN(nc.sbuf_tensor("ident", [128, 128], BF16))
    ones_row = EN(nc.sbuf_tensor("ones_row", [1, 128], BF16))
    ones8 = EN(nc.sbuf_tensor("ones8", [8, 1], F32))
    eoff_sb = EN(nc.sbuf_tensor("eoff_sb", [8, 1], F32))
    bgbr = EN(nc.sbuf_tensor("bgbr", [1, 16], BF16))
    consts = EN(nc.sbuf_tensor("consts", [128, 16], F32))
    xst = EN(nc.sbuf_tensor("xst", [128, 2 * D], F32))
    outst = EN(nc.sbuf_tensor("outst", [128, 2 * D], F32))
    xnf32_b = [outst[:, 0:D], outst[:, D:2 * D]]
    xlot_b = [R1[:, 32896:33920], R1[:, 33920:34944]]
    xlofm = R1[:, 34944:35968]
    lnw = EN(nc.sbuf_tensor("lnw", [128, 96], F32))
    gw48 = EN(nc.sbuf_tensor("gw48", [128, 48], F32))
    rwbuf = EN(nc.sbuf_tensor("rwbuf", [128, 48], F32))
    lnw_b = [lnw[:, :], R1[:, 35968:36160].bitcast(F32)]
    gw48_b = [gw48[:, :], R1[:, 36160:36256].bitcast(F32)]
    rwbuf_b = [rwbuf[:, :], R1[:, 36256:36352].bitcast(F32)]
    gwf = EN(nc.sbuf_tensor("gwf", [128, DC * 8], F32))
    wrtf = EN(nc.sbuf_tensor("wrtf", [128, DC * 8], F32))
    slotf = EN(nc.sbuf_tensor("slotf", [128, CHUNKS], F32))
    slotu = EN(nc.sbuf_tensor("slotu", [128, CHUNKS], U32))
    topw_sc = EN(nc.sbuf_tensor("topw_sc", [128, 8], F32))
    b1st = R1[0:16, 0:NEXP * 128]
    bgbrst = lnw[0:1, 80:96]
    psA = EN(nc.psum_tensor("psA", [128, 1024], F32))
    psB = EN(nc.psum_tensor("psB", [128, 1024], F32))
    psC = EN(nc.psum_tensor("psC", [128, 1024], F32))
    psD = EN(nc.psum_tensor("psD", [128, 1024], F32))
    sem = {k: EN(nc.semaphore("sem_" + k)) for k in ("w", "x", "pe", "act", "dve", "st")}

    ops = {k: [] for k in ("te", "ve", "sc", "gp", "sy")}
    cnt = {k: 0 for k in ("w", "x", "pe", "act", "dve", "st")}

    def I(eng, fn):
        ops[eng].append(["call", fn])

    def W(eng, sname, val):
        if val and val > 0:
            ops[eng].append(["wait", sem[sname], val])

    def INC(eng, sname, amt=1):
        cnt[sname] += amt
        last = ops[eng][-1]
        assert last[0] == "call", f"INC after non-call on {eng}"
        last[0] = "inccall"
        last.append(sem[sname])
        last.append(amt)
        last.append(cnt[sname])
        return cnt[sname]

    def DMA(eng, dst, src, sname, **kw):
        cnt[sname] += 16
        ops[eng].append(["dma", dst, src, sem[sname], kw])
        return cnt[sname]

    def IDMA(eng, dst, dst_off, src, src_off, sname, **kw):
        cnt[sname] += 16
        ops[eng].append(["idma", dst, dst_off, src, src_off, sem[sname], kw])
        return cnt[sname]

    spacer_i = [0]

    def SP():
        c = 4 + spacer_i[0] % 4
        spacer_i[0] += 1
        I("ve", lambda E, c=c: E.tensor_copy(consts[:, c : c + 1], consts[:, 3:4]))

    def VSYNC():
        c = INC("ve", "dve")
        W("ve", "dve", c)
        return c

    w2v = w2sb.rearrange("p (hc d) -> p hc d", d=D)

    def dma_w1q(e, q, buf):
        return DMA("gp", w1q[buf].rearrange("p (dc h) -> p dc h", h=512),
                   w1_h[e].rearrange("(dc p) h -> p dc h", p=128)[:, :, q * 512 : (q + 1) * 512],
                   "w", max_dma_last_dim=2048)

    def dma_w2(e):
        return DMA("gp", w2v, w2_h[e].rearrange("(hc p) d -> p hc d", p=128),
                   "w", max_dma_last_dim=4096)

    def dma_wet(e):
        return DMA("gp", wet_sl[e % 2].rearrange("p (dc o) -> p dc o", o=D),
                   wet_h[e].rearrange("(dc p) o -> p dc o", p=128), "w", max_dma_last_dim=4096)

    I("ve", lambda E: E.memset(consts[:, 0:1], 1e-5))
    I("ve", lambda E: E.memset(consts[:, 1:2], -0.5))
    I("ve", lambda E: E.memset(consts[:, 2:3], -1.0))
    I("ve", lambda E: E.memset(consts[:, 3:4], 1.0 / 1024.0))
    I("ve", lambda E: E.memset(consts[:, 8:9], 0.0))
    I("ve", lambda E: E.memset(consts[:, 9:10], float(CAP + 1)))
    I("ve", lambda E: E.memset(consts[:, 10:11], BIG))
    I("ve", lambda E: E.memset(ones_row[:, :], 1.0))
    I("ve", lambda E: E.memset(ones8[:, :], 1.0))
    I("ve", lambda E: E.memset(gbuf[0], 0.0))
    I("ve", lambda E: E.memset(gbuf[1], 0.0))
    I("ve", lambda E: E.memset(gbuf[2], 0.0))
    VSYNC()

    zview = outst[:, :].bitcast(BF16)[:, 0:ROWW]
    I("ve", lambda E: E.memset(zview, 0.0))
    c_zero_t = VSYNC()
    W("gp", "dve", c_zero_t)
    for zc in range(NSLOT // 128):
        DMA("gp", perm_h[zc * 128 : (zc + 1) * 128, :], zview, "w")
    c_permzero = cnt["w"]
    c_ident = DMA("sy", ident[:, :], idb_h[:, :], "x")
    c_gw = DMA("sy", gwf[:, :].rearrange("p (dc e) -> p dc e", e=8),
               wg_h.rearrange("(dc p) e -> p dc e", p=128), "x")
    c_wrtf = DMA("sy", wrtf[:, :].rearrange("p (dc e) -> p dc e", e=8),
                 wrt_h.rearrange("(dc p) e -> p dc e", p=128), "x")
    c_eoff = DMA("sy", eoff_sb[:, :], eoff_h[:, :], "x")
    DMA("sy", bgbrst[:, 0:8], bg_h.unsqueeze(0), "x")
    c_bgbrst = DMA("sy", bgbrst[:, 8:16], br_h.unsqueeze(0), "x")

    c_b2 = DMA("gp", bias_stack[0:8, :], b2_h[:, :], "w")
    c_be = DMA("gp", bias_stack[8:16, :], be_h[:, :], "w")
    c_b1l = DMA("gp", b1st[:, :].rearrange("hc (e p) -> hc e p", p=128),
                b1_h.rearrange("e (hc p) -> hc e p", p=128), "w")
    W("te", "w", c_b1l)
    W("te", "x", c_ident)
    for e in range(NEXP):
        I("te", lambda E, e=e: E.matmul(psD[:, e * 16 : e * 16 + 16],
                                        b1st[:, e * 128 : (e + 1) * 128],
                                        ident[0:16, 0:16], start=(e == 0), stop=(e == NEXP - 1),
                                        skip_group_check=True))
    c_b1tr = INC("te", "pe")
    W("sc", "pe", c_b1tr)
    I("sc", lambda E: E.copy(b1sb[:, :], psD[:, 0:128]))
    c_b1_done = INC("sc", "act")

    W("ve", "x", max(c_gw, c_bgbrst, c_wrtf))
    gwf_v = gwf[:, :].rearrange("p (dc e) -> p dc e", e=8)
    wrtf_v = wrtf[:, :].rearrange("p (dc e) -> p dc e", e=8)
    gate_v = gate_w[:, :].rearrange("p (dc c) -> p dc c", c=32)
    I("ve", lambda E: E.tensor_copy(gate_v[:, :, 0:8], gwf_v))
    I("ve", lambda E: E.tensor_copy(gate_v[:, :, 8:16], wrtf_v))
    VSYNC()
    I("ve", lambda E: E.tensor_tensor(gate_v[:, :, 16:24], gwf_v, gate_v[:, :, 0:8], op=OP.subtract))
    I("ve", lambda E: E.tensor_tensor(gate_v[:, :, 24:32], wrtf_v, gate_v[:, :, 8:16], op=OP.subtract))
    I("ve", lambda E: E.tensor_copy(bgbr[:, :], bgbrst[:, :]))
    VSYNC()
    c_init_dve = cnt["dve"]

    xst_last = [0, 0]
    pe_attn_done = 0
    dve_res_last = 0
    ps_rel = {"A": ("dve", 0), "B": ("dve", 0), "C": ("dve", 0), "D": ("dve", 0)}
    act_xn_evac = [0] * CHUNKS

    def ps_use(psname):
        W("te", *ps_rel[psname])

    def ps_release(psname, semname, count):
        ps_rel[psname] = (semname, count)

    for s in range(SUPS):
        ph1 = [dict() for _ in range(CHUNKS)]

        def ph1_ln(ci, s=s):
            lnw = lnw_b[ci % 2]; gw48 = gw48_b[ci % 2]; rwbuf = rwbuf_b[ci % 2]
            xnf32 = xnf32_b[ci % 2]; xlot = xlot_b[ci % 2]
            g_ci = s * CHUNKS + ci
            W("sy", "dve", xst_last[ci % 2])
            c_x = DMA("sy", xst[:, (ci % 2) * D : (ci % 2 + 1) * D],
                      x_h[g_ci * 128 : (g_ci + 1) * 128, :], "x")
            xs = xst[:, (ci % 2) * D : (ci % 2 + 1) * D]
            W("ve", "x", c_x)
            if ci == 0:
                W("ve", "st", cnt["st"])
                if s > 0:
                    W("ve", "pe", pe_attn_done)
                else:
                    W("ve", "pe", c_b1tr)
            xh = xn_tok[:, ci * ROWW : ci * ROWW + D]
            I("ve", lambda E, xs=xs: E.bn_stats(lnw[:, 0:6], xs[:, 0:512]))
            I("ve", lambda E, xs=xs: E.bn_stats(lnw[:, 6:12], xs[:, 512:1024]))
            SP()
            SP()
            I("ve", lambda E: E.bn_aggr(lnw[:, 2:4], lnw[:, 0:12]))
            c_stats = INC("ve", "dve")
            W("sc", "dve", c_stats)
            I("sc", lambda E: E.activation(lnw[:, 5:6], lnw[:, 3:4], AF.Ln, bias=consts[:, 0:1]))
            I("sc", lambda E: E.activation(lnw[:, 6:7], lnw[:, 5:6], AF.Exp, scale=consts[:, 1:2]))
            c_rstd = INC("sc", "act")
            W("ve", "act", c_rstd)
            I("ve", lambda E: E.tensor_scalar(lnw[:, 7:8], lnw[:, 2:3], lnw[:, 6:7], consts[:, 2:3],
                                              op0=OP.mult, op1=OP.mult))
            SP()
            I("ve", lambda E, xs=xs, xh=xh: E.tensor_scalar(xh, xs, lnw[:, 6:7], lnw[:, 7:8],
                                                            op0=OP.mult, op1=OP.add))
            I("ve", lambda E, xs=xs: E.tensor_scalar(xnf32[:, :], xs, lnw[:, 6:7], lnw[:, 7:8],
                                                     op0=OP.mult, op1=OP.add))
            SP()
            I("ve", lambda E, xh=xh: E.tensor_tensor(xlot[:, :], xnf32[:, :], xh, op=OP.subtract))
            c_xn_tok = INC("ve", "dve")
            xst_last[ci % 2] = c_xn_tok
            ph1[ci]["xn_tok"] = c_xn_tok

        def ph1_mid(ci, s=s):
            lnw = lnw_b[ci % 2]; gw48 = gw48_b[ci % 2]; rwbuf = rwbuf_b[ci % 2]
            xnf32 = xnf32_b[ci % 2]; xlot = xlot_b[ci % 2]
            xh = xn_tok[:, ci * ROWW : ci * ROWW + D]
            W("te", "dve", ph1[ci]["xn_tok"])
            if s == 0 and ci == 0:
                W("te", "dve", c_init_dve)
                W("te", "act", c_b1_done)
            ps_use("C")
            for dc in range(DC):
                I("te", lambda E, dc=dc, xh=xh: E.matmul(
                    psC[:, dc * 128 : (dc + 1) * 128], xh[:, dc * 128 : (dc + 1) * 128],
                    ident[:, :], start=(dc % 4 == 0), stop=(dc % 4 == 3), skip_group_check=True))
            c_trhi = INC("te", "pe")
            ps_use("D")
            for dc in range(DC):
                I("te", lambda E, dc=dc: E.matmul(
                    psD[:, dc * 128 : (dc + 1) * 128], xlot[:, dc * 128 : (dc + 1) * 128],
                    ident[:, :], start=(dc % 4 == 0), stop=(dc % 4 == 3), skip_group_check=True))
            c_trlo = INC("te", "pe")
            W("sc", "pe", c_trhi)
            if s > 0 and ci == 0:
                W("sc", "dve", dve_res_last)
            xn_dst = xn_fm.rearrange("p (dc n) -> p dc n", n=NSUP)[:, :, ci * 128 : (ci + 1) * 128]
            I("sc", lambda E, xn_dst=xn_dst: E.copy(xn_dst, psC[:, :].rearrange("p (dc n) -> p dc n", n=128)))
            c_xnhi_evac = INC("sc", "act")
            act_xn_evac[ci] = c_xnhi_evac
            ps_release("C", "act", c_xnhi_evac)
            W("sc", "pe", c_trlo)
            I("sc", lambda E: E.copy(xlofm[:, :], psD[:, :]))
            c_xnlo_evac = INC("sc", "act")
            ps_release("D", "act", c_xnlo_evac)
            W("te", "act", c_xnlo_evac)
            ps_use("C")
            for dc in range(DC):
                I("te", lambda E, dc=dc, ci=ci: E.matmul(
                    psC[:, 0:32],
                    xn_fm[:, dc * NSUP + ci * 128 : dc * NSUP + (ci + 1) * 128],
                    gate_w[:, dc * 32 : dc * 32 + 32],
                    start=(dc == 0), stop=False, skip_group_check=True))
            for dc in range(DC):
                I("te", lambda E, dc=dc: E.matmul(
                    psC[:, 32:48], xlofm[:, dc * 128 : (dc + 1) * 128],
                    gate_w[:, dc * 32 : dc * 32 + 16],
                    start=False, stop=False, skip_group_check=True))
            I("te", lambda E: E.matmul(psC[:, 0:16], ones_row[:, :], bgbr[:, :],
                                       start=False, stop=True, skip_group_check=True))
            c_gate = INC("te", "pe")
            W("ve", "pe", c_gate)
            I("ve", lambda E: E.tensor_copy(gw48[:, :], psC[:, 0:48]))
            c_gate_evac = INC("ve", "dve")
            ps_release("C", "dve", c_gate_evac)
            SP()
            I("ve", lambda E: E.tensor_tensor(lnw[:, 16:24], gw48[:, 16:24], gw48[:, 32:40], op=OP.add))
            I("ve", lambda E: E.tensor_tensor(lnw[:, 24:32], gw48[:, 24:32], gw48[:, 40:48], op=OP.add))
            I("ve", lambda E: E.tensor_tensor(lnw[:, 32:40], lnw[:, 16:24], gw48[:, 0:8], op=OP.add))
            I("ve", lambda E: E.tensor_tensor(lnw[:, 40:48], lnw[:, 24:32], gw48[:, 8:16], op=OP.add))
            I("ve", lambda E: E.reduce_max(lnw[:, 48:49], lnw[:, 32:40], axis=AX.X, negate=True))
            I("ve", lambda E: E.reduce_max(lnw[:, 49:50], lnw[:, 40:48], axis=AX.X, negate=True))
            I("ve", lambda E: E.reduce_max(lnw[:, 50:51], lnw[:, 32:40], axis=AX.X))
            c_glog = INC("ve", "dve")
            W("sc", "dve", c_glog)
            I("sc", lambda E: E.activation(lnw[:, 56:64], lnw[:, 32:40], AF.Exp, bias=lnw[:, 48:49]))
            I("sc", lambda E: E.activation(lnw[:, 64:72], lnw[:, 40:48], AF.Exp, bias=lnw[:, 49:50]))
            c_exps = INC("sc", "act")
            W("ve", "act", c_exps)
            I("ve", lambda E: E.reduce_sum(lnw[:, 51:52], lnw[:, 56:64], axis=AX.X))
            I("ve", lambda E: E.reduce_sum(lnw[:, 52:53], lnw[:, 64:72], axis=AX.X))
            I("ve", lambda E: E.reciprocal(lnw[:, 53:54], lnw[:, 51:52]))
            I("ve", lambda E: E.reciprocal(lnw[:, 54:55], lnw[:, 52:53]))
            I("ve", lambda E: E.tensor_scalar(lnw[:, 72:80], lnw[:, 32:40], lnw[:, 50:51], None,
                                              op0=OP.is_equal))
            I("ve", lambda E: E.tensor_scalar(rwbuf[:, 0:8], lnw[:, 64:72], lnw[:, 54:55], None,
                                              op0=OP.mult))
            I("ve", lambda E, ci=ci: E.tensor_scalar(s_sb[:, ci * 16 : ci * 16 + 8], lnw[:, 72:80],
                                                     lnw[:, 53:54], None, op0=OP.mult))
            I("ve", lambda E: E.reduce_max(rwbuf[:, 8:9], rwbuf[:, 0:8], axis=AX.X))
            tw_dst = xn_tok[:, ci * ROWW + D : (ci + 1) * ROWW]
            I("ve", lambda E, tw_dst=tw_dst: E.tensor_scalar(tw_dst, lnw[:, 56:64], consts[:, 8:9],
                                                             lnw[:, 53:54], op0=OP.mult, op1=OP.add))
            I("ve", lambda E: E.tensor_scalar(rwbuf[:, 16:24], rwbuf[:, 0:8], rwbuf[:, 8:9], None,
                                              op0=OP.is_lt))
            SP()
            I("ve", lambda E: E.tensor_tensor(rwbuf[:, 24:32], rwbuf[:, 0:8], rwbuf[:, 16:24], op=OP.mult))
            SP()
            I("ve", lambda E: E.reduce_max(rwbuf[:, 9:10], rwbuf[:, 24:32], axis=AX.X))
            SP()
            I("ve", lambda E: E.tensor_scalar(rwbuf[:, 32:40], rwbuf[:, 0:8], rwbuf[:, 9:10], None,
                                              op0=OP.is_ge))
            SP()
            I("ve", lambda E, ci=ci: E.tensor_tensor(s_sb[:, ci * 16 + 8 : ci * 16 + 16], rwbuf[:, 0:8],
                                                     rwbuf[:, 32:40], op=OP.mult))
            c_sr = INC("ve", "dve")
            ph1[ci]["sr"] = c_sr

        def ph1_tail(ci, s=s):
            W("te", "dve", ph1[ci]["sr"])
            ps_use("C")
            I("te", lambda E, ci=ci: E.matmul(psC[0:16, 512:640], s_sb[:, ci * 16 : ci * 16 + 16],
                                              ident[:, :], start=True, stop=True))
            c_srT = INC("te", "pe")
            W("sc", "pe", c_srT)
            I("sc", lambda E, ci=ci: E.copy(srT[:, ci * 128 : (ci + 1) * 128], psC[0:16, 512:640]))
            c_srT_evac = INC("sc", "act")
            ps_release("C", "act", c_srT_evac)
            W("te", "act", c_srT_evac)
            if s == 0 and ci == 0:
                W("te", "w", max(c_b2, c_be))
            ps_use("D")
            I("te", lambda E, ci=ci: E.matmul(psD[:, 0:512], srT[:, ci * 128 : (ci + 1) * 128],
                                              bias_stack[:, 0:512], start=True, stop=True))
            I("te", lambda E, ci=ci: E.matmul(psD[:, 512:1024], srT[:, ci * 128 : (ci + 1) * 128],
                                              bias_stack[:, 512:1024], start=True, stop=True))
            c_bias = INC("te", "pe")
            W("ve", "pe", c_bias)
            I("ve", lambda E, ci=ci: E.tensor_copy(acc[:, ci * D : (ci + 1) * D], psD[:, :]))
            c_accinit = INC("ve", "dve")
            ps_release("D", "dve", c_accinit)

        dve_ffn = {}
        pe_ffn_done = [0] * NEXP

        def ffn_chunk(e, ci):
            ps3, nm3 = (psA, "A") if (ci + e) % 2 == 0 else (psB, "B")
            ps_use(nm3)
            for dc in range(DC):
                I("te", lambda E, dc=dc, ci=ci, e=e, ps3=ps3: E.matmul(
                    ps3[:, 0:512],
                    xn_fm[:, dc * NSUP + ci * 128 : dc * NSUP + (ci + 1) * 128],
                    wet_sl[e % 2][:, dc * D : dc * D + 512],
                    start=(dc == 0), stop=(dc == DC - 1)))
                I("te", lambda E, dc=dc, ci=ci, e=e, ps3=ps3: E.matmul(
                    ps3[:, 512:1024],
                    xn_fm[:, dc * NSUP + ci * 128 : dc * NSUP + (ci + 1) * 128],
                    wet_sl[e % 2][:, dc * D + 512 : dc * D + 1024],
                    start=(dc == 0), stop=(dc == DC - 1)))
            c_f = INC("te", "pe")
            W("ve", "pe", c_f)
            I("ve", lambda E, ci=ci, e=e, ps3=ps3: E.scalar_tensor_tensor(
                acc[:, ci * D : (ci + 1) * D], ps3[:, :],
                s_sb[:, ci * 16 + 8 + e : ci * 16 + 8 + e + 1],
                acc[:, ci * D : (ci + 1) * D], op0=OP.mult, op1=OP.add))
            dve_ffn[(e, ci)] = INC("ve", "dve")
            ps_release(nm3, "dve", dve_ffn[(e, ci)])

        c_wet = [0] * NEXP
        if s > 0:
            W("gp", "pe", pe_attn_done)
        c_wet[0] = dma_wet(0)
        c_wet[1] = dma_wet(1)
        for cp in range(0, CHUNKS, 2):
            ph1_ln(cp)
            ph1_ln(cp + 1)
            ph1_mid(cp)
            ph1_mid(cp + 1)
            ph1_tail(cp)
            ph1_tail(cp + 1)
            if cp == 0:
                W("te", "w", c_wet[0])
                W("te", "w", c_wet[1])
            W("te", "act", act_xn_evac[cp])
            ffn_chunk(0, cp)
            ffn_chunk(1, cp)
            W("te", "act", act_xn_evac[cp + 1])
            ffn_chunk(0, cp + 1)
            ffn_chunk(1, cp + 1)
        pe_ffn_done[0] = pe_ffn_done[1] = cnt["pe"]
        W("gp", "pe", cnt["pe"])
        c_wet[2] = dma_wet(2)
        c_wet[3] = dma_wet(3)
        I("ve", lambda E: E.tensor_scalar(maskb[:, :], srT[0:8, :], consts[0:8, 8:9], None,
                                          op0=OP.is_gt))
        VSYNC()
        I("ve", lambda E: E.tensor_tensor_scan(pfx[:, :], maskb[:, :], maskb[:, :],
                                               initial=0.0, op0=OP.add, op1=OP.max))
        VSYNC()
        W("ve", *ps_rel["C"])
        W("ve", *ps_rel["D"])
        tv = [psC[0:8, :], psD[0:8, :]]
        for h in range(2):
            cs = slice(h * 1024, (h + 1) * 1024)
            I("ve", lambda E, h=h, cs=cs: E.tensor_scalar(tv[h], pfx[:, cs], consts[0:8, 9:10],
                                                          None, op0=OP.is_ge))
        VSYNC()
        for h in range(2):
            cs = slice(h * 1024, (h + 1) * 1024)
            I("ve", lambda E, h=h, cs=cs: E.tensor_tensor(tv[h], tv[h], maskb[:, cs], op=OP.mult))
        VSYNC()
        I("ve", lambda E: E.scalar_tensor_tensor(pfx[:, :], pfx[:, :], eoff_sb[:, 0:1],
                                                 maskb[:, :], op0=OP.add, op1=OP.mult))
        VSYNC()
        for h in range(2):
            cs = slice(h * 1024, (h + 1) * 1024)
            I("ve", lambda E, h=h, cs=cs: E.scalar_tensor_tensor(pfx[:, cs], tv[h],
                                                                 consts[0:8, 10:11],
                                                                 pfx[:, cs], op0=OP.mult, op1=OP.add))
        c_slotraw = VSYNC()

        def emit_slots_and_scatter():
            nonlocal c_slotu_out, c_scatter_out
            W("te", "dve", c_slotraw)
            W("te", "x", c_eoff)
            ps_use("D")
            for ci in range(CHUNKS):
                I("te", lambda E, ci=ci: E.matmul(psD[:, ci : ci + 1], pfx[:, ci * 128 : (ci + 1) * 128],
                                                  ones8[:, :], start=(ci == 0), stop=(ci == CHUNKS - 1)))
            I("te", lambda E: E.matmul(psD[:, 512:1024], pfx[:, 0:128], pfx[:, 0:512],
                                       start=True, stop=True, skip_group_check=True))
            c_slotmm = INC("te", "pe")
            W("ve", "pe", c_slotmm)
            I("ve", lambda E: E.tensor_copy(slotf[:, :], psD[:, 0:CHUNKS]))
            VSYNC()
            I("ve", lambda E: E.tensor_copy(slotu[:, :], slotf[:, :]))
            c_slotu = VSYNC()
            ps_release("D", "dve", c_slotu)
            ps_release("C", "dve", c_slotu)

            if debug and s == 0:
                W("sy", "dve", c_slotu)
                DMA("sy", dbg_slot_h, slotu[:, :], "st")
                DMA("sy", dbg_s_h, s_sb[:, :], "st")
            W("gp", "dve", c_slotu)
            for ci in range(CHUNKS):
                IDMA("gp", perm_h[:, :], bass.IndirectOffsetOnAxis(ap=slotu[:, ci : ci + 1], axis=0),
                     xn_tok[:, ci * ROWW : (ci + 1) * ROWW], None, "w",
                     bounds_check=NSLOT - 1, oob_is_err=False)
            c_scatter = cnt["w"]

            c_slotu_out = c_slotu
            c_scatter_out = c_scatter
        c_slotu_out = 0
        c_scatter_out = 0
        for e in range(2, NEXP):
            W("te", "w", c_wet[e])
            for ci in range(CHUNKS):
                ffn_chunk(e, ci)
            pe_ffn_done[e] = cnt["pe"]
            if e + 2 < NEXP:
                W("gp", "pe", pe_ffn_done[e])
                c_wet[e + 2] = dma_wet(e + 2)
            if e == 2:
                emit_slots_and_scatter()

        c_w1 = {}
        W("gp", "w", c_scatter_out)
        W("gp", "pe", pe_ffn_done[4])
        c_w1[(0, 0)] = dma_w1q(0, 0, 0)
        c_w1[(0, 1)] = dma_w1q(0, 1, 1)
        W("gp", "pe", pe_ffn_done[6])
        c_w2 = {0: dma_w2(0)}
        c_pch = {}
        W("sy", "pe", pe_ffn_done[NEXP - 1])
        W("sy", "w", c_scatter_out)
        for j in range(6):
            c_pch[(0, j)] = DMA("sy", pchunk[j], perm_h[j * 128 : (j + 1) * 128, :], "x")

        act_relu = {}
        act_pfm = {}
        st_stage = [0, 0]
        dve_tw = [0]

        for e in range(NEXP):
            for j in range(SEGCH):
                g_abs = e * SEGCH + j
                pch = pchunk[g_abs % 6]
                W("te", "x", c_pch[(e, j)])
                psT, nmT = (psA, "A") if j % 2 == 0 else (psB, "B")
                ps_use(nmT)
                for dc in range(DC):
                    I("te", lambda E, dc=dc, pch=pch, psT=psT: E.matmul(
                        psT[:, dc * 128 : (dc + 1) * 128], pch[:, dc * 128 : (dc + 1) * 128],
                        ident[:, :], start=(dc % 4 == 0), stop=(dc % 4 == 3),
                        skip_group_check=True))
                c_tr = INC("te", "pe")
                W("sc", "pe", c_tr)
                pdst = pfm.rearrange("p (dc n) -> p dc n", n=CAP)[:, :, j * 128 : (j + 1) * 128]
                I("sc", lambda E, pdst=pdst, psT=psT: E.copy(
                    pdst, psT[:, :].rearrange("p (dc n) -> p dc n", n=128)))
                act_pfm[(e, j)] = INC("sc", "act")
                ps_release(nmT, "act", act_pfm[(e, j)])
                W("ve", "x", c_pch[(e, j)])
                I("ve", lambda E, j=j, pch=pch: E.tensor_copy(topw_sc[:, j : j + 1],
                                                              pch[:, D : D + 1]))
                dve_tw[0] = INC("ve", "dve")
                gn = g_abs + 6
                ne, nj = gn // SEGCH, gn % SEGCH
                if ne < NEXP:
                    W("sy", "dve", dve_tw[0])
                    W("sy", "act", act_pfm[(e, j)])
                    c_pch[(ne, nj)] = DMA("sy", pchunk[g_abs % 6],
                                          perm_h[gn * 128 : (gn + 1) * 128, :], "x")
            c_m1 = 0
            for q in range(4):
                buf = q % 2
                W("te", "w", c_w1[(e, q)])
                if q == 0:
                    W("te", "act", act_pfm[(e, SEGCH - 1)])
                for hq in range(4):
                    hc = q * 4 + hq
                    ps1, nm1 = (psA, "A") if hc % 2 == 0 else (psB, "B")
                    ps_use(nm1)
                    for dc in range(DC):
                        I("te", lambda E, dc=dc, hq=hq, buf=buf, ps1=ps1: E.matmul(
                            ps1[:, 0:512],
                            w1q[buf][:, dc * 512 + hq * 128 : dc * 512 + (hq + 1) * 128],
                            pfm[:, dc * CAP : dc * CAP + 512],
                            start=(dc == 0), stop=(dc == DC - 1)))
                        I("te", lambda E, dc=dc, hq=hq, buf=buf, ps1=ps1: E.matmul(
                            ps1[:, 512:768],
                            w1q[buf][:, dc * 512 + hq * 128 : dc * 512 + (hq + 1) * 128],
                            pfm[:, dc * CAP + 512 : dc * CAP + 768],
                            start=(dc == 0), stop=(dc == DC - 1)))
                    c_m1 = INC("te", "pe")
                    W("sc", "pe", c_m1)
                    I("sc", lambda E, e=e, hc=hc, ps1=ps1: E.activation(
                        hidseg[:, hc * CAP : (hc + 1) * CAP], ps1[:, 0:CAP], AF.Relu,
                        bias=b1sb[:, e * HC + hc : e * HC + hc + 1]))
                    act_relu[(e, hc)] = INC("sc", "act")
                    ps_release(nm1, "act", act_relu[(e, hc)])
                nq_e, nq = (e, q + 2) if q + 2 < 4 else (e + 1, q - 2)
                if nq_e < NEXP:
                    W("gp", "pe", c_m1)
                    c_w1[(nq_e, nq)] = dma_w1q(nq_e, nq, buf)
            c_m2 = 0
            for j in range(SEGCH):
                ps2, nm2 = (psC, "C") if j % 2 == 0 else (psD, "D")
                if j == 0:
                    W("te", "w", c_w2[e])
                ps_use(nm2)
                for hc in range(HC):
                    if j == 0:
                        W("te", "act", act_relu[(e, hc)])
                    I("te", lambda E, hc=hc, j=j, ps2=ps2: E.matmul(
                        ps2[:, 0:512],
                        hidseg[:, hc * CAP + j * 128 : hc * CAP + (j + 1) * 128],
                        w2sb[:, hc * D : hc * D + 512],
                        start=(hc == 0), stop=(hc == HC - 1)))
                    I("te", lambda E, hc=hc, j=j, ps2=ps2: E.matmul(
                        ps2[:, 512:1024],
                        hidseg[:, hc * CAP + j * 128 : hc * CAP + (j + 1) * 128],
                        w2sb[:, hc * D + 512 : hc * D + 1024],
                        start=(hc == 0), stop=(hc == HC - 1)))
                c_m2 = INC("te", "pe")
                stg = outst[:, (j % 2) * D : (j % 2 + 1) * D]
                W("ve", "pe", c_m2)
                W("ve", "st", st_stage[j % 2])
                I("ve", lambda E, stg=stg, j=j, ps2=ps2: E.tensor_scalar(
                    stg, ps2[:, :], topw_sc[:, j : j + 1], None, op0=OP.mult))
                c_ev = INC("ve", "dve")
                ps_release(nm2, "dve", c_ev)
                W("sc", "dve", c_ev)
                st_stage[j % 2] = DMA("sc",
                                      aperm_h[e * CAP + j * 128 : e * CAP + (j + 1) * 128, :],
                                      stg, "st")
            if e + 1 < NEXP:
                W("gp", "pe", c_m2)
                c_w2[e + 1] = dma_w2(e + 1)
        pe_attn_done = cnt["pe"]
        c_aperm_done = cnt["st"]

        if debug and s == 0:
            W("sy", "st", c_aperm_done)
            DMA("sy", dbg_ap_h, aperm_h[:, :], "st")
        dve_gb = [0, 0, 0]
        for ci in range(CHUNKS):
            g_ci = s * CHUNKS + ci
            if ci == 0:
                W("gp", "st", c_aperm_done)
            W("gp", "dve", dve_gb[ci % 3])
            c_g = IDMA("gp", gbuf[ci % 3], None, aperm_h[:, :],
                       bass.IndirectOffsetOnAxis(ap=slotu[:, ci : ci + 1], axis=0), "w",
                       bounds_check=NSLOT - 1, oob_is_err=False)
            W("sy", "dve", xst_last[ci % 2])
            c_x = DMA("sy", xst[:, (ci % 2) * D : (ci % 2 + 1) * D],
                      x_h[g_ci * 128 : (g_ci + 1) * 128, :], "x")
            xs = xst[:, (ci % 2) * D : (ci % 2 + 1) * D]
            od = outst[:, (ci % 2) * D : (ci % 2 + 1) * D]
            W("ve", "w", c_g)
            W("ve", "x", c_x)
            W("ve", "st", st_stage[ci % 2])
            I("ve", lambda E, ci=ci: E.tensor_tensor(
                acc[:, ci * D : (ci + 1) * D], acc[:, ci * D : (ci + 1) * D],
                gbuf[ci % 3], op=OP.add))
            dve_gb[ci % 3] = INC("ve", "dve")
            SP()
            I("ve", lambda E, xs=xs, od=od, ci=ci: E.tensor_tensor(
                od, xs, acc[:, ci * D : (ci + 1) * D], op=OP.add))
            c_res = INC("ve", "dve")
            xst_last[ci % 2] = c_res
            dve_res_last = c_res
            W("sy", "dve", c_res)
            st_stage[ci % 2] = DMA("sy", out_h[g_ci * 128 : (g_ci + 1) * 128, :], od, "st")

    W("sy", "st", cnt["st"])

    psem = {k: EN(nc.semaphore("psem_" + k)) for k in ("ve", "sc", "gp")} if paranoid else None
    pcnt = {k: 0 for k in ("ve", "sc", "gp")}

    def replay(engobj, key):
        strict = paranoid and key in ("ve", "sc", "gp")
        breg = [None]
        for item in ops[key]:
            kind = item[0]
            if kind == "call":
                inst = item[1](engobj)
                if strict and inst is not None:
                    pcnt[key] += 1
                    inst.then_inc(psem[key], 1)
                    engobj.wait_ge(psem[key], pcnt[key])
            elif kind == "inccall":
                inst = item[1](engobj)
                inst.then_inc(item[2], item[3])
                if strict:
                    engobj.wait_ge(item[2], item[4])
            elif kind == "wait":
                engobj.wait_ge(item[1], item[2])
            elif kind == "dma":
                engobj.dma_start(item[1], item[2], **item[4]).then_inc(item[3], 16)
            elif kind == "idma":
                kw = dict(item[6])
                if "bounds_check" in kw:
                    if breg[0] is None:
                        breg[0] = engobj.to_reg(kw["bounds_check"])
                    kw["bounds_check"] = breg[0]
                engobj.indirect_dma_start(item[1], item[2], item[3], item[4],
                                          **kw).then_inc(item[5], 16)

    block = EN(nc.Block())

    @block.tensor
    def _(te):
        replay(te, "te")

    @block.vector
    def _(ve):
        replay(ve, "ve")

    @block.scalar
    def _(sc):
        replay(sc, "sc")

    @block.gpsimd
    def _(gp):
        replay(gp, "gp")

    @block.sync
    def _(sy):
        replay(sy, "sy")

    ctx.close()
    return nc


_prebuilt = None


def kernel(L0_attn_x, L0_attn_Wg, L0_attn_bg, L0_attn_W1, L0_attn_b1, L0_attn_W2,
           L0_attn_b2, L0_ffn_W_experts, L0_ffn_b_experts, L0_ffn_W_router, L0_ffn_b_router):
    global _prebuilt
    x = np.ascontiguousarray(np.asarray(L0_attn_x, dtype=np.float32).reshape(NTOK, D))
    shards = np.split(x, NCORE, axis=0)
    common = dict(
        wg=np.ascontiguousarray(L0_attn_Wg, dtype=np.float32),
        bg=np.ascontiguousarray(L0_attn_bg, dtype=np.float32),
        w1=np.ascontiguousarray(L0_attn_W1, dtype=np.float32),
        b1=np.ascontiguousarray(L0_attn_b1, dtype=np.float32),
        w2=np.ascontiguousarray(L0_attn_W2, dtype=np.float32),
        b2=np.ascontiguousarray(L0_attn_b2, dtype=np.float32),
        wet=np.ascontiguousarray(np.asarray(L0_ffn_W_experts, dtype=np.float32).transpose(0, 2, 1)),
        be=np.ascontiguousarray(L0_ffn_b_experts, dtype=np.float32),
        wrt=np.ascontiguousarray(np.asarray(L0_ffn_W_router, dtype=np.float32).T),
        br=np.ascontiguousarray(L0_ffn_b_router, dtype=np.float32),
        idb=np.eye(128, dtype=ml_dtypes.bfloat16),
        eoffs=(np.arange(NEXP, dtype=np.float32) * CAP - 1.0).reshape(NEXP, 1),
    )
    if _prebuilt is None:
        _prebuilt = build_program()
    in_maps = [dict(common, x=sh) for sh in shards]
    res = run_bass_kernel_spmd(_prebuilt, in_maps, core_ids=list(range(NCORE)))
    out = np.concatenate([r["out"] for r in res.results], axis=0)
    return out.reshape(2, 16384, D)


if __name__ == "__main__":
    build_program()
    print("built ok")
